# version 05
# baseline (speedup 1.0000x reference)
import numpy as np
import ml_dtypes
from contextlib import ExitStack
import concourse.bass as bass
import concourse.mybir as mybir
from concourse.bass_utils import run_bass_kernel_spmd

F32 = mybir.dt.float32
BF16 = mybir.dt.bfloat16
AF = mybir.ActivationFunctionType
ALU = mybir.AluOpType

NCORES = 8
T = 2048
H = 4096
S_ = 1024
FD = 9984
KC = H // 128
SGN = S_ // 128
FGN = FD // 128
MGN = H // 128
NB = T // 512
NQ = 4
TQ = T // NQ
NHV = 2
THL = T // NHV
EPS = 1e-5
THD = 2.5
NV = 7
ALL_PHASES = ("p0", "gupre", "gubase", "gugu", "s", "down")


def build_program(phases=ALL_PHASES, debug=False):
    ph = set(phases)
    nc = bass.Bass("TRN2", target_bir_lowering=False)

    x_d = nc.dram_tensor("x", [T, H], F32, kind="ExternalInput")
    wupT_d = nc.dram_tensor("wupT", [MGN, 128, KC * 128], F32, kind="ExternalInput")
    ugi_d = nc.dram_tensor("ugi", [SGN, 128, KC * 256], F32, kind="ExternalInput")
    gwt_d = nc.dram_tensor("gwt", [FGN, 128, KC * 128], F32, kind="ExternalInput")
    uwt_d = nc.dram_tensor("uwt", [FGN, 128, KC * 128], F32, kind="ExternalInput")
    dwT_d = nc.dram_tensor("dwT", [MGN, 128, FGN * 128], F32, kind="ExternalInput")
    ws_d = nc.dram_tensor("ws", [MGN, 128, NV * SGN * 128], F32, kind="ExternalInput")
    laT_d = nc.dram_tensor("laT", [128, KC * 32], F32, kind="ExternalInput")
    lbT_d = nc.dram_tensor("lbT", [32, H], F32, kind="ExternalInput")
    g4T_d = nc.dram_tensor("g4T", [128, KC * 4], F32, kind="ExternalInput")
    lp_d = nc.dram_tensor("lp", [128, SGN], F32, kind="ExternalInput")
    disc_d = nc.dram_tensor("disc", [1, 6], F32, kind="ExternalInput")
    eb_d = nc.dram_tensor("eb", [4, 1], F32, kind="ExternalInput")
    id_d = nc.dram_tensor("idm", [128, 128], F32, kind="ExternalInput")
    ones_d = nc.dram_tensor("onesr", [1, 128], F32, kind="ExternalInput")
    out_d = nc.dram_tensor("out", [T, H], F32, kind="ExternalOutput")
    if debug:
        xn_dump = nc.dram_tensor("xn_dump", [T, H], BF16, kind="ExternalOutput")
        xnT_dump = nc.dram_tensor("xnT_dump", [H, T], BF16, kind="ExternalOutput")
        par_dump = nc.dram_tensor("par_dump", [H, T], F32, kind="ExternalOutput")
        h_dump = nc.dram_tensor("h_dump", [FD, T], BF16, kind="ExternalOutput")
        zm_dump = nc.dram_tensor("zm_dump", [32, T], F32, kind="ExternalOutput")
    xn_d = nc.dram_tensor("xn_s", [T, H], BF16, kind="Internal")
    xnT_d = nc.dram_tensor("xnT_s", [H, T], BF16, kind="Internal")
    h_d = nc.dram_tensor("h_s", [FD, T], BF16, kind="Internal")
    sg_d = nc.dram_tensor("sg_s", [FD, T], BF16, kind="Internal")
    par_d = nc.dram_tensor("par_s", [H, T], F32, kind="Internal")

    es = ExitStack()
    sem = lambda n: es.enter_context(nc.semaphore(n))
    sb = lambda n, shp, dt=F32: es.enter_context(nc.sbuf_tensor(n, shp, dt))

    with es:
        lp_sb = sb("lp_sb", [128, SGN])
        disc_sb = sb("disc_sb", [1, 6])
        nd_sb = sb("nd_sb", [128, 6])
        msc_sb = sb("msc_sb", [128, 6])
        eb_sb = sb("eb_sb", [4, 1])
        id_sb = sb("id_sb", [128, 128])
        ones_sb = sb("ones_sb", [1, 128])

        s_ini = sem("s_ini"); s_pre = sem("s_pre"); s_bc = sem("s_bc")
        s_xld = sem("s_xld"); s_sq = sem("s_sq"); s_vpe = sem("s_vpe")
        s_sd = sem("s_sd"); s_lnv = sem("s_lnv"); s_xnst = sem("s_xnst")
        s_xnT = sem("s_xnT"); s_xnTst = sem("s_xnTst"); s_wa = sem("s_wa")
        s_mmL = sem("s_mmL"); s_evL = sem("s_evL"); s_mmZ = sem("s_mmZ"); s_zm = sem("s_zm")
        s_wB = sem("s_wB"); s_mmB = sem("s_mmB"); s_evB = sem("s_evB"); s_pstB = sem("s_pstB")
        s_wG = sem("s_wG"); s_mmG = sem("s_mmG"); s_aG = sem("s_aG")
        s_dG = sem("s_dG"); s_hst = sem("s_hst")
        c_p0 = sem("c_p0"); c_pre = sem("c_pre"); s_nrm = sem("s_nrm")
        s_sgst = sem("s_sgst"); s_wU = sem("s_wU"); s_mmU = sem("s_mmU")
        s_sgl = sem("s_sgl"); s_dU = sem("s_dU")
        s_sems = {k: sem(f"sq_{k}") for k in
                  ["xq", "ug", "ws", "m1", "e1", "cld", "adt", "ddt", "lnd",
                   "m3", "e3", "acc", "chv"]}
        PERQ = {"xq": 64, "ug": 16 * SGN, "ws": 16 * MGN, "m1": SGN, "e1": SGN,
                "cld": SGN, "adt": 6 * SGN, "ddt": 6 * SGN, "lnd": SGN,
                "m3": MGN, "e3": MGN, "acc": 16 * MGN, "chv": 0}
        d_sems = {k: sem(f"sd_{k}") for k in
                  ["hld", "dw", "pld", "md", "ed", "tr", "en", "out"]}
        PERD = {"hld": 32, "dw": 16 * MGN, "pld": 16 * MGN, "md": MGN, "ed": MGN,
                "tr": MGN, "en": MGN, "out": 64}

        with nc.psum_tensor("pI", [128, 6], F32) as pI, nc.Block() as blk:
            @blk.sync
            def _(sy):
                sy.dma_start(out=lp_sb[:, :], in_=lp_d[:, :]).then_inc(s_ini, 16)
                sy.dma_start(out=disc_sb[:, :], in_=disc_d[:, :]).then_inc(s_ini, 16)
                sy.dma_start(out=eb_sb[:, :], in_=eb_d[:, :]).then_inc(s_ini, 16)
                sy.dma_start(out=id_sb[:, :], in_=id_d[:, :]).then_inc(s_ini, 16)
                sy.dma_start(out=ones_sb[:, :], in_=ones_d[:, :]).then_inc(s_ini, 16)
                sy.wait_ge(s_ini, 16 * 5)

            @blk.scalar
            def _(s):
                s.dma_start(out=out_d[:, :], in_=x_d[:, :]).then_inc(s_pre, 16)

            @blk.tensor
            def _(te):
                te.wait_ge(s_ini, 16 * 5)
                te.matmul(pI[:, 0:6], ones_sb[0:1, 0:128], disc_sb[0:1, 0:6],
                          start=True, stop=True).then_inc(s_bc, 1)

            @blk.vector
            def _(v):
                for j in range(6):
                    v.memset(msc_sb[:, j:j + 1], float(j + 1))
                v.wait_ge(s_bc, 1)
                v.tensor_scalar_mul(nd_sb[:, 0:6], pI[:, 0:6], -1.0)

        NT = T // 128
        if "p0" in ph:
            with ExitStack() as e0:
                sb0 = lambda n, shp, dt=F32: e0.enter_context(nc.sbuf_tensor(n, shp, dt))
                xin = [sb0(f"xin{i}", [128, H]) for i in range(3)]
                epsc = sb0("epsc", [128, 1])
                xnb = [sb0(f"xnb{i}", [128, H], BF16) for i in range(2)]
                dump = sb0("dump", [128, H])
                ssumt = sb0("ssumt", [128, NT]); ssqt = sb0("ssqt", [128, NT])
                mut = sb0("mut", [128, NT]); mu2t = sb0("mu2t", [128, NT])
                vpet = sb0("vpet", [128, NT]); sdt = sb0("sdt", [128, NT])
                rstdt = sb0("rstdt", [128, NT]); nmb = sb0("nmb", [128, NT])
                with nc.Block() as blk:
                    @blk.sync
                    def _(sy):
                        for i in range(NT):
                            if i >= 3:
                                sy.wait_ge(s_lnv, i - 2)
                            sy.dma_start(out=xin[i % 3][:, :],
                                         in_=x_d[i * 128:(i + 1) * 128, :]).then_inc(s_xld, 16)
                            if i >= 1:
                                sy.wait_ge(s_lnv, i)
                                sy.dma_start(out=xn_d[(i - 1) * 128:i * 128, :],
                                             in_=xnb[(i - 1) % 2][:, :]).then_inc(s_xnst, 16)
                        sy.wait_ge(s_lnv, NT)
                        sy.dma_start(out=xn_d[(NT - 1) * 128:NT * 128, :],
                                     in_=xnb[(NT - 1) % 2][:, :]).then_inc(s_xnst, 16)
                        sy.wait_ge(s_xnst, 16 * NT)
                        sy.wait_ge(s_xld, 16 * NT)

                    @blk.scalar
                    def _(s):
                        for i in range(NT):
                            s.wait_ge(s_xld, 16 * (i + 1))
                            s.activation(dump[:, :], xin[i % 3][:, :], AF.Square,
                                         accum_out=ssqt[:, i:i + 1]).then_inc(s_sq, 1)
                            s.wait_ge(s_vpe, i + 1)
                            s.activation(sdt[:, i:i + 1], vpet[:, i:i + 1], AF.Sqrt,
                                         bias=epsc[:, 0:1]).then_inc(s_sd, 1)
                            s.wait_ge(s_nrm, i + 1)
                            if i >= 2:
                                s.wait_ge(s_xnst, 16 * (i - 1))
                            s.activation(xnb[i % 2][:, :], xin[i % 3][:, :], AF.Identity,
                                         bias=nmb[:, i:i + 1],
                                         scale=rstdt[:, i:i + 1]).then_inc(s_lnv, 1)

                    @blk.vector
                    def _(v):
                        cc = 0
                        v.memset(epsc[:, :], EPS)
                        for i in range(NT):
                            v.wait_ge(s_xld, 16 * (i + 1))
                            v.reduce_sum(ssumt[:, i:i + 1], xin[i % 3][:, :],
                                         axis=mybir.AxisListType.X).then_inc(c_p0, 1)
                            cc += 1; v.wait_ge(c_p0, cc)
                            v.tensor_scalar_mul(mut[:, i:i + 1], ssumt[:, i:i + 1],
                                                1.0 / H).then_inc(c_p0, 1)
                            cc += 1; v.wait_ge(c_p0, cc)
                            v.tensor_mul(mu2t[:, i:i + 1], mut[:, i:i + 1],
                                         mut[:, i:i + 1]).then_inc(c_p0, 1)
                            cc += 1; v.wait_ge(c_p0, cc)
                            v.wait_ge(s_sq, i + 1)
                            v.scalar_tensor_tensor(vpet[:, i:i + 1], ssqt[:, i:i + 1],
                                                   1.0 / H, mu2t[:, i:i + 1],
                                                   ALU.mult, ALU.subtract).then_inc(s_vpe, 1)
                            v.wait_ge(s_vpe, i + 1)
                            v.wait_ge(s_sd, i + 1)
                            v.reciprocal(rstdt[:, i:i + 1], sdt[:, i:i + 1]).then_inc(c_p0, 1)
                            cc += 1; v.wait_ge(c_p0, cc)
                            v.scalar_tensor_tensor(nmb[:, i:i + 1], mut[:, i:i + 1], -1.0,
                                                   rstdt[:, i:i + 1], ALU.mult,
                                                   ALU.mult).then_inc(s_nrm, 1)

        with ExitStack() as eGS:
            zm_sb = eGS.enter_context(nc.sbuf_tensor("zm_sb", [32, T], BF16))
            lb_sb = eGS.enter_context(nc.sbuf_tensor("lb_sb", [32, H], BF16))
            with ExitStack() as eG:
                xnT_sb = eG.enter_context(nc.sbuf_tensor("xnT_sb", [128, KC * T], BF16))

                def xnT(k, nb):
                    return xnT_sb[:, k * T + nb * 512: k * T + (nb + 1) * 512]

                if "gupre" in ph:
                    with ExitStack() as eP:
                        sbP = lambda n, shp, dt=F32: eP.enter_context(nc.sbuf_tensor(n, shp, dt))
                        g4A_sb = sbP("g4A_sb", [128, KC * 4], BF16)
                        laA_sb = sbP("laA_sb", [128, KC * 32], BF16)
                        l4 = sbP("l4", [4, T]); l4b = sbP("l4b", [4, T]); l4m = sbP("l4m", [4, T])
                        oh32 = sbP("oh32", [32, T]); oh32b = sbP("oh32b", [32, T])
                        with nc.psum_tensor("pZ", [128, T], F32) as pZ, nc.Block() as blk:
                            @blk.sync
                            def _(sy):
                                for c in range(KC):
                                    sy.dma_start(out=xnT_sb[:, c * T:(c + 1) * T],
                                                 in_=xn_d[:, c * 128:(c + 1) * 128],
                                                 transpose=True).then_inc(s_xnT, 16)
                                sy.wait_ge(s_xnT, 16 * KC)
                                sy.dma_start(out=xnT_d.rearrange("(c p) n -> p c n", p=128),
                                             in_=xnT_sb.rearrange("p (c n) -> p c n", c=KC)
                                             ).then_inc(s_xnTst, 16)
                                sy.wait_ge(s_xnTst, 16)

                            @blk.gpsimd
                            def _(g):
                                g.dma_start(out=g4A_sb[:, :], in_=g4T_d[:, :]).then_inc(s_wa, 16)
                                g.dma_start(out=laA_sb[:, :], in_=laT_d[:, :]).then_inc(s_wa, 16)
                                g.dma_start(out=lb_sb[:, :], in_=lbT_d[:, :]).then_inc(s_wa, 16)
                                g.wait_ge(s_wa, 48)

                            @blk.tensor
                            def _(te):
                                te.wait_ge(s_xnT, 16 * KC)
                                te.wait_ge(s_wa, 32)
                                for nb in range(NB):
                                    for k in range(KC):
                                        ins = te.matmul(pZ[0:4, nb * 512:(nb + 1) * 512],
                                                        g4A_sb[:, k * 4:(k + 1) * 4],
                                                        xnT(k, nb),
                                                        start=(k == 0), stop=(k == KC - 1))
                                ins.then_inc(s_mmL, 1)
                                te.wait_ge(s_evL, 1)
                                for nb in range(NB):
                                    for k in range(KC):
                                        ins = te.matmul(pZ[0:32, nb * 512:(nb + 1) * 512],
                                                        laA_sb[:, k * 32:(k + 1) * 32],
                                                        xnT(k, nb),
                                                        start=(k == 0), stop=(k == KC - 1))
                                ins.then_inc(s_mmZ, 1)

                            @blk.vector
                            def _(v):
                                cc = 0
                                def bump(ins):
                                    nonlocal cc
                                    cc += 1
                                    ins.then_inc(c_pre, 1)
                                    v.wait_ge(c_pre, cc)
                                v.wait_ge(s_mmL, 1)
                                v.tensor_scalar(l4[:, :], pZ[0:4, :], eb_sb[0:4, 0:1], None,
                                                ALU.add).then_inc(s_evL, 1)
                                v.wait_ge(s_evL, 1)
                                idm = list(range(4, 32))
                                bump(v.stream_shuffle(l4b[:, :], l4[:, :], [1, 2, 3, 0] + idm))
                                bump(v.tensor_max(l4m[:, :], l4[:, :], l4b[:, :]))
                                bump(v.stream_shuffle(l4b[:, :], l4m[:, :], [2, 3, 0, 1] + idm))
                                bump(v.tensor_max(l4m[:, :], l4m[:, :], l4b[:, :]))
                                bump(v.stream_shuffle(l4b[:, :], l4m[:, :], [0, 0, 0, 0] + idm))
                                bump(v.tensor_tensor(oh32[0:4, :], l4[:, :], l4b[0:4, :],
                                                     ALU.is_ge))
                                bump(v.stream_shuffle(oh32b[:, :], oh32[:, :],
                                                      [0] * 8 + [1] * 8 + [2] * 8 + [3] * 8))
                                v.wait_ge(s_mmZ, 1)
                                v.tensor_mul(zm_sb[0:32, :], pZ[0:32, :],
                                             oh32b[0:32, :]).then_inc(s_zm, 1)

                if "gubase" in ph:
                    with ExitStack() as eB:
                        sbB = lambda n, shp, dt=F32: eB.enter_context(nc.sbuf_tensor(n, shp, dt))
                        wb_sb = [sbB(f"wb{i}", [128, KC * 128], BF16) for i in range(2)]
                        bs_sb = [sbB(f"bs{i}", [128, T]) for i in range(2)]
                        with nc.psum_tensor("pB0", [128, T], F32) as pB0, \
                             nc.psum_tensor("pB1", [128, T], F32) as pB1, nc.Block() as blk:
                            pB = [pB0, pB1]

                            @blk.gpsimd
                            def _(g):
                                for m in range(MGN):
                                    if m >= 2:
                                        g.wait_ge(s_mmB, m - 1)
                                    g.dma_start(out=wb_sb[m % 2][:, :],
                                                in_=wupT_d[m]).then_inc(s_wB, 16)
                                g.wait_ge(s_wB, 16 * MGN)

                            @blk.tensor
                            def _(te):
                                for m in range(MGN):
                                    te.wait_ge(s_wB, 16 * (m + 1))
                                    if m >= 2:
                                        te.wait_ge(s_evB, m - 1)
                                    for k in range(KC):
                                        for nb in range(NB):
                                            ins = te.matmul(
                                                pB[m % 2][:, nb * 512:(nb + 1) * 512],
                                                wb_sb[m % 2][:, k * 128:(k + 1) * 128],
                                                xnT(k, nb),
                                                start=(k == 0), stop=(k == KC - 1))
                                    ins.then_inc(s_mmB, 1)

                            @blk.vector
                            def _(v):
                                for m in range(MGN):
                                    v.wait_ge(s_mmB, m + 1)
                                    if m >= 2:
                                        v.wait_ge(s_pstB, 16 * (m - 1))
                                    v.tensor_copy(bs_sb[m % 2][:, :],
                                                  pB[m % 2][:, :]).then_inc(s_evB, 1)

                            @blk.sync
                            def _(sy):
                                for m in range(MGN):
                                    sy.wait_ge(s_evB, m + 1)
                                    sy.dma_start(out=par_d[m * 128:(m + 1) * 128, :],
                                                 in_=bs_sb[m % 2][:, :]).then_inc(s_pstB, 16)
                                sy.wait_ge(s_pstB, 16 * MGN)

                if "gugu" in ph:
                    with ExitStack() as eU:
                        sbU = lambda n, shp, dt=F32: eU.enter_context(nc.sbuf_tensor(n, shp, dt))
                        wg_sb = [sbU(f"wg{i}", [128, KC * 128], BF16) for i in range(2)]
                        gsb = [sbU(f"gsb{i}", [128, T], BF16) for i in range(2)]
                        with nc.psum_tensor("pG0", [128, T], F32) as pG0, \
                             nc.psum_tensor("pG1", [128, T], F32) as pG1, nc.Block() as blk:
                            pG = [pG0, pG1]

                            @blk.gpsimd
                            def _(g):
                                for m in range(FGN):
                                    if m >= 2:
                                        g.wait_ge(s_mmG, m - 1)
                                    g.dma_start(out=wg_sb[m % 2][:, :],
                                                in_=gwt_d[m]).then_inc(s_wG, 16)
                                g.wait_ge(s_wG, 16 * FGN)

                            @blk.tensor
                            def _(te):
                                for m in range(FGN):
                                    te.wait_ge(s_wG, 16 * (m + 1))
                                    if m >= 2:
                                        te.wait_ge(s_aG, m - 1)
                                    for k in range(KC):
                                        for nb in range(NB):
                                            ins = te.matmul(
                                                pG[m % 2][:, nb * 512:(nb + 1) * 512],
                                                wg_sb[m % 2][:, k * 128:(k + 1) * 128],
                                                xnT(k, nb),
                                                start=(k == 0), stop=(k == KC - 1))
                                    ins.then_inc(s_mmG, 1)

                            @blk.scalar
                            def _(s):
                                for m in range(FGN):
                                    s.wait_ge(s_mmG, m + 1)
                                    if m >= 2:
                                        s.wait_ge(s_sgst, 16 * (m - 1))
                                    s.activation(gsb[m % 2][:, :], pG[m % 2][:, :],
                                                 AF.Silu).then_inc(s_aG, 1)

                            @blk.sync
                            def _(sy):
                                for m in range(FGN):
                                    sy.wait_ge(s_aG, m + 1)
                                    sy.dma_start(out=sg_d[m * 128:(m + 1) * 128, :],
                                                 in_=gsb[m % 2][:, :]).then_inc(s_sgst, 16)
                                sy.wait_ge(s_sgst, 16 * FGN)

                if "gugu" in ph:
                    with ExitStack() as eU:
                        sbU = lambda n, shp, dt=F32: eU.enter_context(nc.sbuf_tensor(n, shp, dt))
                        wu_sb = [sbU(f"wu{i}", [128, KC * 128], BF16) for i in range(2)]
                        sgl = [sbU(f"sgl{i}", [128, T], BF16) for i in range(2)]
                        h_sb = [sbU(f"hsb{i}", [128, T], BF16) for i in range(2)]
                        with nc.psum_tensor("pU0", [128, T], F32) as pU0, \
                             nc.psum_tensor("pU1", [128, T], F32) as pU1, nc.Block() as blk:
                            pU = [pU0, pU1]

                            @blk.gpsimd
                            def _(g):
                                for m in range(FGN):
                                    if m >= 2:
                                        g.wait_ge(s_mmU, m - 1)
                                    g.dma_start(out=wu_sb[m % 2][:, :],
                                                in_=uwt_d[m]).then_inc(s_wU, 16)
                                g.wait_ge(s_wU, 16 * FGN)

                            @blk.sync
                            def _(sy):
                                for m in range(FGN):
                                    if m >= 2:
                                        sy.wait_ge(s_dU, m - 1)
                                    sy.dma_start(out=sgl[m % 2][:, :],
                                                 in_=sg_d[m * 128:(m + 1) * 128, :]
                                                 ).then_inc(s_sgl, 16)
                                    if m >= 1:
                                        sy.wait_ge(s_dU, m)
                                        sy.dma_start(out=h_d[(m - 1) * 128:m * 128, :],
                                                     in_=h_sb[(m - 1) % 2][:, :]
                                                     ).then_inc(s_hst, 16)
                                sy.wait_ge(s_dU, FGN)
                                sy.dma_start(out=h_d[(FGN - 1) * 128:FGN * 128, :],
                                             in_=h_sb[(FGN - 1) % 2][:, :]).then_inc(s_hst, 16)
                                sy.wait_ge(s_hst, 16 * FGN)
                                sy.wait_ge(s_sgl, 16 * FGN)

                            @blk.tensor
                            def _(te):
                                for m in range(FGN):
                                    te.wait_ge(s_wU, 16 * (m + 1))
                                    if m >= 2:
                                        te.wait_ge(s_dU, m - 1)
                                    for k in range(KC):
                                        for nb in range(NB):
                                            ins = te.matmul(
                                                pU[m % 2][:, nb * 512:(nb + 1) * 512],
                                                wu_sb[m % 2][:, k * 128:(k + 1) * 128],
                                                xnT(k, nb),
                                                start=(k == 0), stop=(k == KC - 1))
                                    ins.then_inc(s_mmU, 1)

                            @blk.vector
                            def _(v):
                                for m in range(FGN):
                                    v.wait_ge(s_mmU, m + 1)
                                    v.wait_ge(s_sgl, 16 * (m + 1))
                                    if m >= 2:
                                        v.wait_ge(s_hst, 16 * (m - 1))
                                    v.tensor_mul(h_sb[m % 2][:, :], sgl[m % 2][:, :],
                                                 pU[m % 2][:, :]).then_inc(s_dU, 1)

            if "s" in ph:
                with ExitStack() as eS:
                    sbS = lambda n, shp, dt=F32: eS.enter_context(nc.sbuf_tensor(n, shp, dt))
                    xq_sb = sbS("xq_sb", [128, KC * TQ], BF16)
                    ug_sb = [sbS(f"ug{i}", [128, KC * 256], BF16) for i in range(2)]
                    tmp_sb = sbS("tmp_sb", [128, NV * SGN * TQ], BF16)
                    ws_sb = [sbS(f"wssb{i}", [128, NV * SGN * 128], BF16) for i in range(3)]
                    delta = sbS("delta", [128, TQ]); absd = sbS("absd", [128, TQ])
                    ldl = sbS("ldl", [128, TQ]); gesb = sbS("gesb", [128, TQ])
                    negf = sbS("negf", [128, TQ]); keep = sbS("keep", [128, TQ])
                    au_f = sbS("au_f", [128, TQ]); au_s = sbS("au_s", [128, TQ])
                    au_k = sbS("au_k", [128, TQ]); au_sk = sbS("au_sk", [128, TQ])
                    dt_sc = [sbS(f"dtsc{i}", [128, TQ]) for i in range(2)]
                    fst = [sbS(f"fst{i}", [128, TQ]) for i in range(2)]
                    selmap = {1: au_s, 2: au_f, 3: au_s, 4: au_f, 5: au_sk, 6: au_k}
                    with nc.psum_tensor("pS1u", [128, 2 * TQ], F32) as pS1u, \
                         nc.psum_tensor("pS1g", [128, 2 * TQ], F32) as pS1g, \
                         nc.psum_tensor("pF", [128, 2 * TQ], F32) as pF:
                        s_cc = {"v": 0}
                        with nc.Block() as blk:
                            for q in range(NQ):
                                sq = s_sems
                                qo = {k: q * PERQ[k] for k in PERQ}
                                @blk.sync
                                def _(sy, q=q, sq=sq, qo=qo):
                                    if q >= 1:
                                        sy.wait_ge(sq["m1"], q * PERQ["m1"])
                                    for piece in range(4):
                                        sy.dma_start(
                                            out=xq_sb.rearrange("p (c n) -> p c n", c=KC)
                                            [:, piece * 8:(piece + 1) * 8, :],
                                            in_=xnT_d.rearrange("(c p) n -> p c n", p=128)
                                            [:, piece * 8:(piece + 1) * 8,
                                             q * TQ:(q + 1) * TQ]).then_inc(sq["xq"], 16)
                                    if q == NQ - 1:
                                        sy.wait_ge(sq["xq"], qo["xq"] + 64)

                                @blk.gpsimd
                                def _(g, q=q, sq=sq, qo=qo):
                                    for sg in range(SGN):
                                        w = qo["m1"] + sg - 1
                                        if w > 0:
                                            g.wait_ge(sq["m1"], w)
                                        g.dma_start(out=ug_sb[sg % 2][:, :],
                                                    in_=ugi_d[sg]).then_inc(sq["ug"], 16)
                                    for m in range(3):
                                        w = qo["m3"] + m - 2
                                        if w > 0:
                                            g.wait_ge(sq["m3"], w)
                                        g.dma_start(out=ws_sb[m % 3][:, :],
                                                    in_=ws_d[m]).then_inc(sq["ws"], 16)
                                    for m in range(MGN):
                                        if m + 3 < MGN:
                                            g.wait_ge(sq["m3"], qo["m3"] + m + 1)
                                            g.dma_start(out=ws_sb[m % 3][:, :],
                                                        in_=ws_d[m + 3]).then_inc(sq["ws"], 16)
                                        g.wait_ge(sq["e3"], qo["e3"] + m + 1)
                                        g.dma_start(out=par_d[m * 128:(m + 1) * 128,
                                                              q * TQ:(q + 1) * TQ],
                                                    in_=fst[m % 2][:, :],
                                                    accum_op=ALU.add).then_inc(sq["acc"], 16)
                                    if q == NQ - 1:
                                        g.wait_ge(sq["acc"], qo["acc"] + 16 * MGN)
                                        g.wait_ge(sq["ws"], qo["ws"] + 16 * MGN)
                                        g.wait_ge(sq["ug"], qo["ug"] + 16 * SGN)

                                @blk.tensor
                                def _(te, q=q, sq=sq, qo=qo):
                                    for sg in range(SGN):
                                        te.wait_ge(sq["ug"], qo["ug"] + 16 * (sg + 1))
                                        w = qo["e1"] + sg - 1
                                        if w > 0:
                                            te.wait_ge(sq["e1"], w)
                                        par = (sg % 2) * TQ
                                        for k in range(KC):
                                            if sg == 0 and k % 8 == 0:
                                                te.wait_ge(sq["xq"], qo["xq"] + 16 * (k // 8 + 1))
                                            ins = te.matmul(
                                                pS1u[:, par:par + TQ],
                                                ug_sb[sg % 2][:, k * 256:k * 256 + 128],
                                                xq_sb[:, k * TQ:(k + 1) * TQ],
                                                start=(k == 0), stop=(k == KC - 1))
                                        for k in range(KC):
                                            ins = te.matmul(
                                                pS1g[:, par:par + TQ],
                                                ug_sb[sg % 2][:, k * 256 + 128:(k + 1) * 256],
                                                xq_sb[:, k * TQ:(k + 1) * TQ],
                                                start=(k == 0), stop=(k == KC - 1))
                                        ins.then_inc(sq["m1"], 1)
                                    te.wait_ge(sq["ddt"], qo["ddt"] + 6 * SGN)
                                    te.wait_ge(s_zm, 1)
                                    for m in range(MGN):
                                        te.wait_ge(sq["ws"], qo["ws"] + 16 * (m + 1))
                                        w = qo["e3"] + m - 1
                                        if w > 0:
                                            te.wait_ge(sq["e3"], w)
                                        par = (m % 2) * TQ
                                        for c in range(NV * SGN):
                                            te.matmul(pF[:, par:par + TQ],
                                                      ws_sb[m % 3][:, c * 128:(c + 1) * 128],
                                                      tmp_sb[:, c * TQ:(c + 1) * TQ],
                                                      start=(c == 0), stop=False)
                                        te.matmul(pF[:, par:par + TQ],
                                                  lb_sb[0:32, m * 128:(m + 1) * 128],
                                                  zm_sb[0:32, q * TQ:(q + 1) * TQ],
                                                  start=False, stop=True).then_inc(sq["m3"], 1)

                                @blk.scalar
                                def _(s, q=q, sq=sq, qo=qo):
                                    for sg in range(SGN):
                                        s.wait_ge(sq["cld"], qo["cld"] + sg + 1)
                                        s.activation(ldl[:, :], absd[:, :],
                                                     AF.Ln).then_inc(sq["lnd"], 1)
                                        s.wait_ge(sq["lnd"], qo["lnd"] + sg + 1)
                                        for v_ in range(1, 7):
                                            w = max(0, 6 * sg + v_ - 2)
                                            if qo["ddt"] + w > 0:
                                                s.wait_ge(sq["ddt"], qo["ddt"] + w)
                                            s.activation(dt_sc[v_ % 2][:, :], ldl[:, :], AF.Exp,
                                                         bias=nd_sb[:, v_ - 1:v_],
                                                         scale=msc_sb[:, v_ - 1:v_]
                                                         ).then_inc(sq["adt"], 1)

                                @blk.vector
                                def _(v, q=q, sq=sq, qo=qo):
                                    if q >= 1:
                                        v.wait_ge(sq["m3"], q * PERQ["m3"])
                                    def bump(ins):
                                        s_cc["v"] += 1
                                        ins.then_inc(sq["chv"], 1)
                                        v.wait_ge(sq["chv"], qo["chv"] + s_cc["v"])
                                    for sg in range(SGN):
                                        par = (sg % 2) * TQ
                                        v.wait_ge(sq["m1"], qo["m1"] + sg + 1)
                                        bump(v.tensor_scalar(delta[:, :], pS1g[:, par:par + TQ],
                                                             lp_sb[:, sg:sg + 1], None,
                                                             ALU.subtract))
                                        v.tensor_copy(au_f[:, :], pS1u[:, par:par + TQ]
                                                      ).then_inc(sq["e1"], 1)
                                        v.wait_ge(sq["e1"], qo["e1"] + sg + 1)
                                        v.tensor_copy(tmp_sb[:, (0 * SGN + sg) * TQ:
                                                             (0 * SGN + sg + 1) * TQ],
                                                      au_f[:, :])
                                        if sg >= 1:
                                            v.wait_ge(sq["lnd"], qo["lnd"] + sg)
                                        v.scalar_tensor_tensor(absd[:, :], delta[:, :], -1.0,
                                                               delta[:, :], ALU.mult, ALU.max
                                                               ).then_inc(sq["cld"], 1)
                                        v.wait_ge(sq["cld"], qo["cld"] + sg + 1)
                                        bump(v.tensor_single_scalar(gesb[:, :], delta[:, :],
                                                                    0.0, ALU.is_gt))
                                        bump(v.tensor_scalar(negf[:, :], gesb[:, :], 2.0, -1.0,
                                                             ALU.mult, ALU.add))
                                        bump(v.tensor_single_scalar(keep[:, :], absd[:, :],
                                                                    THD, ALU.is_le))
                                        bump(v.tensor_mul(au_s[:, :], au_f[:, :], negf[:, :]))
                                        bump(v.tensor_mul(au_k[:, :], au_f[:, :], keep[:, :]))
                                        bump(v.tensor_mul(au_sk[:, :], au_s[:, :], keep[:, :]))
                                        for v_ in range(1, 7):
                                            v.wait_ge(sq["adt"], qo["adt"] + 6 * sg + v_)
                                            v.tensor_mul(tmp_sb[:, (v_ * SGN + sg) * TQ:
                                                                (v_ * SGN + sg + 1) * TQ],
                                                         dt_sc[v_ % 2][:, :],
                                                         selmap[v_][:, :]).then_inc(sq["ddt"], 1)
                                    for m in range(MGN):
                                        v.wait_ge(sq["m3"], qo["m3"] + m + 1)
                                        w = qo["acc"] + 16 * (m - 1)
                                        if w > 0:
                                            v.wait_ge(sq["acc"], w)
                                        par = (m % 2) * TQ
                                        v.tensor_copy(fst[m % 2][:, :], pF[:, par:par + TQ]
                                                      ).then_inc(sq["e3"], 1)

        if "down" in ph:
            with ExitStack() as eD:
                sbD = lambda n, shp, dt=F32: eD.enter_context(nc.sbuf_tensor(n, shp, dt))
                hh_sb = sbD("hh_sb", [128, FGN * TQ], BF16)
                dw_sb = [sbD(f"dwsb{i}", [128, FGN * 128], BF16) for i in range(4)]
                pp_sb = sbD("pp_sb", [128, TQ])
                ot_sb = sbD("ot_sb", [128, TQ])
                oct_sb = [sbD(f"oct{i}", [128, 4 * 1024]) for i in range(2)]
                with nc.psum_tensor("pD", [128, 2 * TQ], F32) as pD, \
                     nc.psum_tensor("pN", [128, 2 * TQ], F32) as pN:
                    with nc.Block() as blk:
                      for qd in range(4):
                        sd = d_sems
                        do = {k: qd * PERD[k] for k in PERD}
                        if True:
                            @blk.sync
                            def _(sy, qd=qd, sd=sd, do=do):
                                if qd >= 1:
                                    sy.wait_ge(sd["md"], qd * PERD["md"])
                                for jj in range(2):
                                    sy.dma_start(
                                        out=hh_sb.rearrange("p (c n) -> p c n", c=FGN)
                                        [:, jj * 39:(jj + 1) * 39, :],
                                        in_=h_d.rearrange("(c p) n -> p c n", p=128)
                                        [:, jj * 39:(jj + 1) * 39,
                                         qd * TQ:(qd + 1) * TQ]).then_inc(sd["hld"], 16)
                                for m in range(MGN):
                                    if do["ed"] + m > 0:
                                        sy.wait_ge(sd["ed"], do["ed"] + m)
                                    sy.dma_start(out=pp_sb[:, :],
                                                 in_=par_d[m * 128:(m + 1) * 128,
                                                           qd * TQ:(qd + 1) * TQ]
                                                 ).then_inc(sd["pld"], 16)
                                if qd == 3:
                                    sy.wait_ge(sd["pld"], do["pld"] + 16 * MGN)
                                    sy.wait_ge(sd["hld"], do["hld"] + 32)

                            @blk.gpsimd
                            def _(g, qd=qd, sd=sd, do=do):
                                def dwload(m):
                                    w = do["md"] + m - 3
                                    if w > 0:
                                        g.wait_ge(sd["md"], w)
                                    g.dma_start(out=dw_sb[(qd * MGN + m) % 4][:, :],
                                                in_=dwT_d[m]).then_inc(sd["dw"], 16)
                                for m in range(4):
                                    dwload(m)
                                if qd == 0:
                                    g.wait_ge(s_pre, 16)
                                for m in range(MGN):
                                    if m + 4 < MGN:
                                        dwload(m + 4)
                                    if m % 8 == 7:
                                        o = m // 8
                                        g.wait_ge(sd["en"], do["en"] + m + 1)
                                        g.dma_start(
                                            out=out_d.rearrange("(a j p) h -> a p j h",
                                                                j=4, p=128)
                                            [qd][:, :, o * 1024:(o + 1) * 1024],
                                            in_=oct_sb[o % 2].rearrange("p (j h) -> p j h",
                                                                        j=4),
                                            accum_op=ALU.add).then_inc(sd["out"], 16)
                                if qd == 3:
                                    g.wait_ge(sd["out"], do["out"] + 16 * 4)
                                    g.wait_ge(sd["dw"], do["dw"] + 16 * MGN)

                            @blk.tensor
                            def _(te, qd=qd, sd=sd, do=do):
                                te.wait_ge(sd["hld"], do["hld"] + 16)
                                for m in range(MGN):
                                    te.wait_ge(sd["dw"], do["dw"] + 16 * (m + 1))
                                    par = (m % 2) * TQ
                                    if do["ed"] + m - 1 > 0:
                                        te.wait_ge(sd["ed"], do["ed"] + m - 1)
                                    for k in range(FGN):
                                        if m == 0 and k == 39:
                                            te.wait_ge(sd["hld"], do["hld"] + 32)
                                        ins = te.matmul(
                                            pD[:, par:par + TQ],
                                            dw_sb[(qd * MGN + m) % 4][:, k * 128:(k + 1) * 128],
                                            hh_sb[:, k * TQ:(k + 1) * TQ],
                                            start=(k == 0), stop=(k == FGN - 1))
                                    ins.then_inc(sd["md"], 1)
                                    if m >= 1:
                                        mm = m - 1
                                        te.wait_ge(sd["ed"], do["ed"] + mm + 1)
                                        if do["en"] + mm - 1 > 0:
                                            te.wait_ge(sd["en"], do["en"] + mm - 1)
                                        parn = (mm % 2) * TQ
                                        for j in range(4):
                                            ins = te.transpose(
                                                pN[:, parn + j * 128:parn + (j + 1) * 128],
                                                ot_sb[:, j * 128:(j + 1) * 128], id_sb[:, :])
                                        ins.then_inc(sd["tr"], 1)
                                mm = MGN - 1
                                te.wait_ge(sd["ed"], do["ed"] + mm + 1)
                                if do["en"] + mm - 1 > 0:
                                    te.wait_ge(sd["en"], do["en"] + mm - 1)
                                parn = (mm % 2) * TQ
                                for j in range(4):
                                    ins = te.transpose(pN[:, parn + j * 128:parn + (j + 1) * 128],
                                                       ot_sb[:, j * 128:(j + 1) * 128],
                                                       id_sb[:, :])
                                ins.then_inc(sd["tr"], 1)

                            @blk.vector
                            def _(v, qd=qd, sd=sd, do=do):
                                for m in range(MGN):
                                    par = (m % 2) * TQ
                                    v.wait_ge(sd["md"], do["md"] + m + 1)
                                    v.wait_ge(sd["pld"], do["pld"] + 16 * (m + 1))
                                    if do["tr"] + m > 0:
                                        v.wait_ge(sd["tr"], do["tr"] + m)
                                    v.tensor_add(ot_sb[:, :], pD[:, par:par + TQ],
                                                 pp_sb[:, :]).then_inc(sd["ed"], 1)
                                    v.wait_ge(sd["tr"], do["tr"] + m + 1)
                                    o = m // 8
                                    if m % 8 == 0 and do["out"] + 16 * (o - 1) > 0:
                                        v.wait_ge(sd["out"], do["out"] + 16 * (o - 1))
                                    parn = (m % 2) * TQ
                                    v.tensor_copy(
                                        oct_sb[o % 2].rearrange("p (j h) -> p j h", j=4)
                                        [:, :, (m % 8) * 128:(m % 8 + 1) * 128],
                                        pN[:, parn:parn + TQ].rearrange("p (j c) -> p j c",
                                                                        j=4)
                                        ).then_inc(sd["en"], 1)

        with nc.Block() as blk:
            @blk.gpsimd
            def _(g):
                if "down" in ph:
                    g.wait_ge(d_sems["out"], 4 * 64)
                else:
                    g.wait_ge(s_pre, 16)
                if debug:
                    dsem = nc.semaphore("s_dbg").__enter__()
                    n_dbg = 0
                    if "p0" in ph:
                        g.dma_start(out=xn_dump[:, :], in_=xn_d[:, :]).then_inc(dsem, 16)
                        n_dbg += 1
                    if "gupre" in ph:
                        g.dma_start(out=xnT_dump[:, :], in_=xnT_d[:, :]).then_inc(dsem, 16)
                        n_dbg += 1
                        g.dma_start(out=zm_dump[:, :], in_=zm_sb[:, :]).then_inc(dsem, 16)
                        n_dbg += 1
                    if "gubase" in ph or "s" in ph:
                        g.dma_start(out=par_dump[:, :], in_=par_d[:, :]).then_inc(dsem, 16)
                        n_dbg += 1
                    if "gugu" in ph:
                        g.dma_start(out=h_dump[:, :], in_=h_d[:, :]).then_inc(dsem, 16)
                        n_dbg += 1
                    g.wait_ge(dsem, 16 * n_dbg)

    return nc


def _prep_inputs(L0_attn_hidden_states, L0_attn_up_w, L0_attn_gate_w,
                 L0_attn_local_point, L0_attn_local_approx_output,
                 L0_attn_fuse_weight, L0_attn_discount_factor,
                 L0_attn_ffn_gate_w, L0_attn_ffn_up_w, L0_attn_ffn_down_w,
                 L0_ffn_W_up, L0_ffn_gate_w, L0_ffn_expert_bias,
                 L0_ffn_lora_A, L0_ffn_lora_B):
    f32 = np.float32
    asc = np.ascontiguousarray

    def tile_panels(SRC, W):
        Ktot, Ntot = SRC.shape
        kc, nm = Ktot // 128, Ntot // W
        return asc(SRC.reshape(kc, 128, nm, W).transpose(2, 1, 0, 3).reshape(nm, 128, kc * W))

    x = asc(np.asarray(L0_attn_hidden_states, f32).reshape(NCORES * T, H))
    wupT = tile_panels(np.asarray(L0_ffn_W_up, f32).T, 128)
    upT = np.asarray(L0_attn_up_w, f32).T
    gateT = np.asarray(L0_attn_gate_w, f32).T
    ugi = tile_panels(np.stack([upT.reshape(H, SGN, 128), gateT.reshape(H, SGN, 128)],
                               axis=2).reshape(H, 2 * S_), 256)
    gwt = tile_panels(np.asarray(L0_attn_ffn_gate_w, f32).T, 128)
    uwt = tile_panels(np.asarray(L0_attn_ffn_up_w, f32).T, 128)
    dwT = tile_panels(np.asarray(L0_attn_ffn_down_w, f32).T, 128)
    laoT = np.asarray(L0_attn_local_approx_output, f32).T
    fuseT = np.asarray(L0_attn_fuse_weight, f32).transpose(2, 1, 0)
    ws = tile_panels(np.concatenate([laoT[None], fuseT], axis=0).reshape(NV * S_, H), 128)
    laT = tile_panels(np.asarray(L0_ffn_lora_A, f32).reshape(32, H).T, 32)[0]
    lbT = asc(np.asarray(L0_ffn_lora_B, f32).transpose(0, 2, 1).reshape(32, H))
    g4T = tile_panels(np.asarray(L0_ffn_gate_w, f32).T, 4)[0]
    lp = asc(np.asarray(L0_attn_local_point, f32).reshape(SGN, 128).T)
    disc = asc(np.asarray(L0_attn_discount_factor, f32).reshape(1, 6))
    eb = asc(np.asarray(L0_ffn_expert_bias, f32).reshape(4, 1))
    idm = np.eye(128, dtype=f32)
    onesr = np.ones((1, 128), dtype=f32)
    shared = dict(wupT=wupT, ugi=ugi, gwt=gwt, uwt=uwt, dwT=dwT, ws=ws, laT=laT, lbT=lbT,
                  g4T=g4T, lp=lp, disc=disc, eb=eb, idm=idm, onesr=onesr)
    in_maps = []
    for c in range(NCORES):
        d = dict(shared)
        d["x"] = asc(x[c * T:(c + 1) * T, :])
        in_maps.append(d)
    return in_maps


def kernel(**inputs):
    in_maps = _prep_inputs(**inputs)
    nc = build_program()
    res = run_bass_kernel_spmd(nc, in_maps, core_ids=list(range(NCORES)))
    outs = [np.asarray(r["out"]) for r in res.results]
    full = np.concatenate(outs, axis=0)
    return full.reshape(4, 4096, 4096).astype(np.float32)


# version 10
# speedup vs baseline: 1.1845x; 1.1845x over previous
import numpy as np
import ml_dtypes
from contextlib import ExitStack
import concourse.bass as bass
import concourse.mybir as mybir
from concourse.bass_utils import run_bass_kernel_spmd

F32 = mybir.dt.float32
BF16 = mybir.dt.bfloat16
AF = mybir.ActivationFunctionType
ALU = mybir.AluOpType

NCORES = 8
T = 2048
H = 4096
S_ = 1024
FD = 9984
KC = H // 128
SGN = S_ // 128
FGN = FD // 128
MGN = H // 128
NB = T // 512
NQ = 4
TQ = T // NQ
NHV = 2
THL = T // NHV
EPS = 1e-5
THD = 2.5
NV = 7
ALL_PHASES = ("p0", "gupre", "gubase", "gugu", "s", "down")


def build_program(phases=ALL_PHASES, debug=False):
    ph = set(phases)
    nc = bass.Bass("TRN2", target_bir_lowering=False)

    x_d = nc.dram_tensor("x", [T, H], F32, kind="ExternalInput")
    wupT_d = nc.dram_tensor("wupT", [MGN, 128, KC * 128], F32, kind="ExternalInput")
    ugi_d = nc.dram_tensor("ugi", [SGN, 128, KC * 256], F32, kind="ExternalInput")
    gwt_d = nc.dram_tensor("gwt", [FGN, 128, KC * 128], F32, kind="ExternalInput")
    uwt_d = nc.dram_tensor("uwt", [FGN, 128, KC * 128], F32, kind="ExternalInput")
    dwT_d = nc.dram_tensor("dwT", [MGN, 128, FGN * 128], F32, kind="ExternalInput")
    ws_d = nc.dram_tensor("ws", [MGN, 128, NV * SGN * 128], F32, kind="ExternalInput")
    laT_d = nc.dram_tensor("laT", [128, KC * 32], F32, kind="ExternalInput")
    lbT_d = nc.dram_tensor("lbT", [32, H], F32, kind="ExternalInput")
    g4T_d = nc.dram_tensor("g4T", [128, KC * 4], F32, kind="ExternalInput")
    lp_d = nc.dram_tensor("lp", [128, SGN], F32, kind="ExternalInput")
    disc_d = nc.dram_tensor("disc", [1, 6], F32, kind="ExternalInput")
    eb_d = nc.dram_tensor("eb", [4, 1], F32, kind="ExternalInput")
    id_d = nc.dram_tensor("idm", [128, 128], F32, kind="ExternalInput")
    ones_d = nc.dram_tensor("onesr", [1, 128], F32, kind="ExternalInput")
    out_d = nc.dram_tensor("out", [T, H], F32, kind="ExternalOutput")
    if debug:
        xn_dump = nc.dram_tensor("xn_dump", [T, H], BF16, kind="ExternalOutput")
        xnT_dump = nc.dram_tensor("xnT_dump", [H, T], BF16, kind="ExternalOutput")
        par_dump = nc.dram_tensor("par_dump", [H, T], F32, kind="ExternalOutput")
        h_dump = nc.dram_tensor("h_dump", [FD, T], BF16, kind="ExternalOutput")
        zm_dump = nc.dram_tensor("zm_dump", [32, T], F32, kind="ExternalOutput")
    xn_d = nc.dram_tensor("xn_s", [T, H], BF16, kind="Internal")
    xnT_d = nc.dram_tensor("xnT_s", [NQ, 128, KC, TQ], BF16, kind="Internal")
    h_d = nc.dram_tensor("h_s", [NQ, 128, FGN, TQ], BF16, kind="Internal")
    sg_d = nc.dram_tensor("sg_s", [FD, T], BF16, kind="Internal")
    par_d = nc.dram_tensor("par_s", [H, T], F32, kind="Internal")

    es = ExitStack()
    sem = lambda n: es.enter_context(nc.semaphore(n))
    sb = lambda n, shp, dt=F32: es.enter_context(nc.sbuf_tensor(n, shp, dt))

    with es:
        lp_sb = sb("lp_sb", [128, SGN])
        disc_sb = sb("disc_sb", [1, 6])
        nd_sb = sb("nd_sb", [128, 6])
        msc_sb = sb("msc_sb", [128, 6])
        eb_sb = sb("eb_sb", [4, 1])
        id_sb = sb("id_sb", [128, 128])
        ones_sb = sb("ones_sb", [1, 128])

        s_ini = sem("s_ini"); s_pre = sem("s_pre"); s_bc = sem("s_bc")
        s_xld = sem("s_xld"); s_sq = sem("s_sq"); s_vpe = sem("s_vpe")
        s_sd = sem("s_sd"); s_lnv = sem("s_lnv"); s_xnst = sem("s_xnst")
        s_xnT = sem("s_xnT"); s_xnTst = sem("s_xnTst"); s_wa = sem("s_wa")
        s_mmL = sem("s_mmL"); s_evL = sem("s_evL"); s_mmZ = sem("s_mmZ"); s_zm = sem("s_zm")
        s_wB = sem("s_wB"); s_mmB = sem("s_mmB"); s_evB = sem("s_evB"); s_pstB = sem("s_pstB")
        s_wG = sem("s_wG"); s_mmG = sem("s_mmG"); s_aG = sem("s_aG")
        s_dG = sem("s_dG"); s_hst = sem("s_hst")
        c_p0 = sem("c_p0"); c_pre = sem("c_pre"); s_nrm = sem("s_nrm")
        sx_sl = [sem(f"sx{j}") for j in range(6)]
        sst_sl = [sem(f"sst{j}") for j in range(2)]
        swb_sl = [sem(f"swb{j}") for j in range(2)]
        spb_sl = [sem(f"spb{j}") for j in range(2)]
        swg_sl = [sem(f"swg{j}") for j in range(2)]
        ssg_sl = [sem(f"ssg{j}") for j in range(2)]
        swu_sl = [sem(f"swu{j}") for j in range(2)]
        sgl_sl = [sem(f"sgl{j}") for j in range(2)]
        shs_sl = [sem(f"shs{j}") for j in range(2)]
        sxq_sl = [sem(f"sxq{j}") for j in range(4)]
        sug_sl = [sem(f"sug{j}") for j in range(2)]
        sws_sl = [sem(f"sws{j}") for j in range(3)]
        sac_sl = [sem(f"sac{j}") for j in range(2)]
        shl_sl = [sem(f"shl{j}") for j in range(2)]
        sdw_sl = [sem(f"sdw{j}") for j in range(4)]
        sou_sl = [sem(f"sou{j}") for j in range(2)]
        s_sgst = sem("s_sgst"); s_wU = sem("s_wU"); s_mmU = sem("s_mmU")
        s_sgl = sem("s_sgl"); s_dU = sem("s_dU")
        s_sems = {k: sem(f"sq_{k}") for k in
                  ["xq", "ug", "ws", "m1", "e1", "cld", "adt", "ddt", "lnd",
                   "m3", "e3", "acc", "chv"]}
        PERQ = {"xq": 64, "ug": 16 * SGN, "ws": 16 * MGN, "m1": SGN, "e1": SGN,
                "cld": SGN, "adt": 6 * SGN, "ddt": 6 * SGN, "lnd": SGN,
                "m3": MGN, "e3": MGN, "acc": 16 * MGN, "chv": 0}
        d_sems = {k: sem(f"sd_{k}") for k in
                  ["hld", "dw", "pld", "md", "ed", "tr", "en", "out"]}
        PERD = {"hld": 32, "dw": 16 * MGN, "pld": 16 * MGN, "md": MGN, "ed": MGN,
                "tr": MGN, "en": MGN, "out": 64}

        with nc.psum_tensor("pI", [128, 6], F32) as pI, nc.Block() as blk:
            @blk.sync
            def _(sy):
                sy.dma_start(out=lp_sb[:, :], in_=lp_d[:, :]).then_inc(s_ini, 16)
                sy.dma_start(out=disc_sb[:, :], in_=disc_d[:, :]).then_inc(s_ini, 16)
                sy.dma_start(out=eb_sb[:, :], in_=eb_d[:, :]).then_inc(s_ini, 16)
                sy.dma_start(out=id_sb[:, :], in_=id_d[:, :]).then_inc(s_ini, 16)
                sy.dma_start(out=ones_sb[:, :], in_=ones_d[:, :]).then_inc(s_ini, 16)
                sy.wait_ge(s_ini, 16 * 5)

            @blk.scalar
            def _(s):
                s.dma_start(out=out_d[:, :], in_=x_d[:, :]).then_inc(s_pre, 16)

            @blk.tensor
            def _(te):
                te.wait_ge(s_ini, 16 * 5)
                te.matmul(pI[:, 0:6], ones_sb[0:1, 0:128], disc_sb[0:1, 0:6],
                          start=True, stop=True).then_inc(s_bc, 1)

            @blk.vector
            def _(v):
                for j in range(6):
                    v.memset(msc_sb[:, j:j + 1], float(j + 1))
                v.wait_ge(s_bc, 1)
                v.tensor_scalar_mul(nd_sb[:, 0:6], pI[:, 0:6], -1.0)

        NT = T // 128
        if "p0" in ph:
            with ExitStack() as e0:
                sb0 = lambda n, shp, dt=F32: e0.enter_context(nc.sbuf_tensor(n, shp, dt))
                xin = [sb0(f"xin{i}", [128, H]) for i in range(6)]
                epsc = sb0("epsc", [128, 1])
                xnb = [sb0(f"xnb{i}", [128, H], BF16) for i in range(2)]
                dump = sb0("dump", [128, H])
                ssumt = sb0("ssumt", [128, NT]); ssqt = sb0("ssqt", [128, NT])
                mut = sb0("mut", [128, NT]); mu2t = sb0("mu2t", [128, NT])
                vpet = sb0("vpet", [128, NT]); sdt = sb0("sdt", [128, NT])
                rstdt = sb0("rstdt", [128, NT]); nmb = sb0("nmb", [128, NT])
                with nc.Block() as blk:
                    @blk.sync
                    def _(sy):
                        for i in range(NT):
                            if i >= 6:
                                sy.wait_ge(s_lnv, i - 5)
                            sy.dma_start(out=xin[i % 6][:, :],
                                         in_=x_d[i * 128:(i + 1) * 128, :]
                                         ).then_inc(sx_sl[i % 6], 16)
                            if i >= 1:
                                sy.wait_ge(s_lnv, i)
                                sy.dma_start(out=xn_d[(i - 1) * 128:i * 128, :],
                                             in_=xnb[(i - 1) % 2][:, :]
                                             ).then_inc(sst_sl[(i - 1) % 2], 16)
                        sy.wait_ge(s_lnv, NT)
                        sy.dma_start(out=xn_d[(NT - 1) * 128:NT * 128, :],
                                     in_=xnb[(NT - 1) % 2][:, :]
                                     ).then_inc(sst_sl[(NT - 1) % 2], 16)
                        for j in range(2):
                            sy.wait_ge(sst_sl[j], 16 * (NT // 2))
                        for j in range(6):
                            sy.wait_ge(sx_sl[j], 16 * ((NT - 1 - j) // 6 + 1))

                    @blk.scalar
                    def _(s):
                        for i in range(NT):
                            s.wait_ge(sx_sl[i % 6], 16 * (i // 6 + 1))
                            s.activation(dump[:, :], xin[i % 6][:, :], AF.Square,
                                         accum_out=ssqt[:, i:i + 1]).then_inc(s_sq, 1)
                            s.wait_ge(s_vpe, i + 1)
                            s.activation(sdt[:, i:i + 1], vpet[:, i:i + 1], AF.Sqrt,
                                         bias=epsc[:, 0:1]).then_inc(s_sd, 1)
                            s.wait_ge(s_nrm, i + 1)
                            if i >= 2:
                                s.wait_ge(sst_sl[i % 2], 16 * ((i - 2) // 2 + 1))
                            s.activation(xnb[i % 2][:, :], xin[i % 6][:, :], AF.Identity,
                                         bias=nmb[:, i:i + 1],
                                         scale=rstdt[:, i:i + 1]).then_inc(s_lnv, 1)

                    @blk.vector
                    def _(v):
                        cc = 0
                        v.memset(epsc[:, :], EPS)
                        for i in range(NT):
                            v.wait_ge(sx_sl[i % 6], 16 * (i // 6 + 1))
                            v.reduce_sum(ssumt[:, i:i + 1], xin[i % 6][:, :],
                                         axis=mybir.AxisListType.X).then_inc(c_p0, 1)
                            cc += 1; v.wait_ge(c_p0, cc)
                            v.tensor_scalar_mul(mut[:, i:i + 1], ssumt[:, i:i + 1],
                                                1.0 / H).then_inc(c_p0, 1)
                            cc += 1; v.wait_ge(c_p0, cc)
                            v.tensor_mul(mu2t[:, i:i + 1], mut[:, i:i + 1],
                                         mut[:, i:i + 1]).then_inc(c_p0, 1)
                            cc += 1; v.wait_ge(c_p0, cc)
                            v.wait_ge(s_sq, i + 1)
                            v.scalar_tensor_tensor(vpet[:, i:i + 1], ssqt[:, i:i + 1],
                                                   1.0 / H, mu2t[:, i:i + 1],
                                                   ALU.mult, ALU.subtract).then_inc(s_vpe, 1)
                            v.wait_ge(s_vpe, i + 1)
                            v.wait_ge(s_sd, i + 1)
                            v.reciprocal(rstdt[:, i:i + 1], sdt[:, i:i + 1]).then_inc(c_p0, 1)
                            cc += 1; v.wait_ge(c_p0, cc)
                            v.scalar_tensor_tensor(nmb[:, i:i + 1], mut[:, i:i + 1], -1.0,
                                                   rstdt[:, i:i + 1], ALU.mult,
                                                   ALU.mult).then_inc(s_nrm, 1)

        with ExitStack() as eGS:
            zm_sb = eGS.enter_context(nc.sbuf_tensor("zm_sb", [32, T], BF16))
            lb_sb = eGS.enter_context(nc.sbuf_tensor("lb_sb", [32, H], BF16))
            with ExitStack() as eG:
                xnT_sb = eG.enter_context(nc.sbuf_tensor("xnT_sb", [128, KC * T], BF16))

                def xnT(k, nb):
                    return xnT_sb[:, k * T + nb * 512: k * T + (nb + 1) * 512]

                if "gupre" in ph:
                    with ExitStack() as eP:
                        sbP = lambda n, shp, dt=F32: eP.enter_context(nc.sbuf_tensor(n, shp, dt))
                        g4A_sb = sbP("g4A_sb", [128, KC * 4], BF16)
                        laA_sb = sbP("laA_sb", [128, KC * 32], BF16)
                        l4 = sbP("l4", [4, T]); l4b = sbP("l4b", [4, T]); l4m = sbP("l4m", [4, T])
                        oh32 = sbP("oh32", [32, T]); oh32b = sbP("oh32b", [32, T])
                        with nc.psum_tensor("pZ", [128, T], F32) as pZ, nc.Block() as blk:
                            @blk.sync
                            def _(sy):
                                for c in range(KC):
                                    sy.dma_start(out=xnT_sb[:, c * T:(c + 1) * T],
                                                 in_=xn_d[:, c * 128:(c + 1) * 128],
                                                 transpose=True).then_inc(s_xnT, 16)
                                sy.wait_ge(s_xnT, 16 * KC)
                                for qq in range(NQ):
                                    sy.dma_start(
                                        out=xnT_d[qq],
                                        in_=xnT_sb.rearrange("p (c n) -> p c n", c=KC)
                                        [:, :, qq * TQ:(qq + 1) * TQ]).then_inc(s_xnTst, 16)
                                sy.wait_ge(s_xnTst, 64)

                            @blk.gpsimd
                            def _(g):
                                g.dma_start(out=g4A_sb[:, :], in_=g4T_d[:, :]).then_inc(s_wa, 16)
                                g.dma_start(out=laA_sb[:, :], in_=laT_d[:, :]).then_inc(s_wa, 16)
                                g.dma_start(out=lb_sb[:, :], in_=lbT_d[:, :]).then_inc(s_wa, 16)
                                g.wait_ge(s_wa, 48)

                            @blk.tensor
                            def _(te):
                                te.wait_ge(s_xnT, 16 * KC)
                                te.wait_ge(s_wa, 48)
                                for nb in range(NB):
                                    for k in range(KC):
                                        ins = te.matmul(pZ[0:4, nb * 512:(nb + 1) * 512],
                                                        g4A_sb[:, k * 4:(k + 1) * 4],
                                                        xnT(k, nb),
                                                        start=(k == 0), stop=(k == KC - 1))
                                ins.then_inc(s_mmL, 1)
                                te.wait_ge(s_evL, 1)
                                for nb in range(NB):
                                    for k in range(KC):
                                        ins = te.matmul(pZ[0:32, nb * 512:(nb + 1) * 512],
                                                        laA_sb[:, k * 32:(k + 1) * 32],
                                                        xnT(k, nb),
                                                        start=(k == 0), stop=(k == KC - 1))
                                ins.then_inc(s_mmZ, 1)

                            @blk.vector
                            def _(v):
                                cc = 0
                                def bump(ins):
                                    nonlocal cc
                                    cc += 1
                                    ins.then_inc(c_pre, 1)
                                    v.wait_ge(c_pre, cc)
                                v.wait_ge(s_mmL, 1)
                                v.tensor_scalar(l4[:, :], pZ[0:4, :], eb_sb[0:4, 0:1], None,
                                                ALU.add).then_inc(s_evL, 1)
                                v.wait_ge(s_evL, 1)
                                idm = list(range(4, 32))
                                bump(v.stream_shuffle(l4b[:, :], l4[:, :], [1, 2, 3, 0] + idm))
                                bump(v.tensor_max(l4m[:, :], l4[:, :], l4b[:, :]))
                                bump(v.stream_shuffle(l4b[:, :], l4m[:, :], [2, 3, 0, 1] + idm))
                                bump(v.tensor_max(l4m[:, :], l4m[:, :], l4b[:, :]))
                                bump(v.stream_shuffle(l4b[:, :], l4m[:, :], [0, 0, 0, 0] + idm))
                                bump(v.tensor_tensor(oh32[0:4, :], l4[:, :], l4b[0:4, :],
                                                     ALU.is_ge))
                                bump(v.stream_shuffle(oh32b[:, :], oh32[:, :],
                                                      [0] * 8 + [1] * 8 + [2] * 8 + [3] * 8))
                                v.wait_ge(s_mmZ, 1)
                                v.tensor_mul(zm_sb[0:32, :], pZ[0:32, :],
                                             oh32b[0:32, :]).then_inc(s_zm, 1)

                if "gubase" in ph:
                    with ExitStack() as eB:
                        sbB = lambda n, shp, dt=F32: eB.enter_context(nc.sbuf_tensor(n, shp, dt))
                        wb_sb = [sbB(f"wb{i}", [128, KC * 128], BF16) for i in range(2)]
                        bs_sb = [sbB(f"bs{i}", [128, T]) for i in range(2)]
                        with nc.psum_tensor("pB0", [128, T], F32) as pB0, \
                             nc.psum_tensor("pB1", [128, T], F32) as pB1, nc.Block() as blk:
                            pB = [pB0, pB1]

                            @blk.gpsimd
                            def _(g):
                                for m in range(MGN):
                                    if m >= 2:
                                        g.wait_ge(s_mmB, m - 1)
                                    g.dma_start(out=wb_sb[m % 2][:, :],
                                                in_=wupT_d[m]).then_inc(swb_sl[m % 2], 16)
                                for j in range(2):
                                    g.wait_ge(swb_sl[j], 16 * (MGN // 2))

                            @blk.tensor
                            def _(te):
                                for m in range(MGN):
                                    te.wait_ge(swb_sl[m % 2], 16 * (m // 2 + 1))
                                    if m >= 2:
                                        te.wait_ge(s_evB, m - 1)
                                    for k in range(KC):
                                        for nb in range(NB):
                                            ins = te.matmul(
                                                pB[m % 2][:, nb * 512:(nb + 1) * 512],
                                                wb_sb[m % 2][:, k * 128:(k + 1) * 128],
                                                xnT(k, nb),
                                                start=(k == 0), stop=(k == KC - 1))
                                    ins.then_inc(s_mmB, 1)

                            @blk.vector
                            def _(v):
                                for m in range(MGN):
                                    v.wait_ge(s_mmB, m + 1)
                                    if m >= 2:
                                        v.wait_ge(spb_sl[m % 2], 16 * ((m - 2) // 2 + 1))
                                    v.tensor_copy(bs_sb[m % 2][:, :],
                                                  pB[m % 2][:, :]).then_inc(s_evB, 1)

                            @blk.sync
                            def _(sy):
                                for m in range(MGN):
                                    sy.wait_ge(s_evB, m + 1)
                                    sy.dma_start(out=par_d[m * 128:(m + 1) * 128, :],
                                                 in_=bs_sb[m % 2][:, :]
                                                 ).then_inc(spb_sl[m % 2], 16)
                                for j in range(2):
                                    sy.wait_ge(spb_sl[j], 16 * (MGN // 2))

                if "gugu" in ph:
                    with ExitStack() as eU:
                        sbU = lambda n, shp, dt=F32: eU.enter_context(nc.sbuf_tensor(n, shp, dt))
                        wg_sb = [sbU(f"wg{i}", [128, KC * 128], BF16) for i in range(2)]
                        gsb = [sbU(f"gsb{i}", [128, T], BF16) for i in range(2)]
                        with nc.psum_tensor("pG0", [128, T], F32) as pG0, \
                             nc.psum_tensor("pG1", [128, T], F32) as pG1, nc.Block() as blk:
                            pG = [pG0, pG1]

                            @blk.gpsimd
                            def _(g):
                                for m in range(FGN):
                                    if m >= 2:
                                        g.wait_ge(s_mmG, m - 1)
                                    g.dma_start(out=wg_sb[m % 2][:, :],
                                                in_=gwt_d[m]).then_inc(swg_sl[m % 2], 16)
                                for j in range(2):
                                    g.wait_ge(swg_sl[j], 16 * ((FGN - 1 - j) // 2 + 1))

                            @blk.tensor
                            def _(te):
                                for m in range(FGN):
                                    te.wait_ge(swg_sl[m % 2], 16 * (m // 2 + 1))
                                    if m >= 2:
                                        te.wait_ge(s_aG, m - 1)
                                    for k in range(KC):
                                        for nb in range(NB):
                                            ins = te.matmul(
                                                pG[m % 2][:, nb * 512:(nb + 1) * 512],
                                                wg_sb[m % 2][:, k * 128:(k + 1) * 128],
                                                xnT(k, nb),
                                                start=(k == 0), stop=(k == KC - 1))
                                    ins.then_inc(s_mmG, 1)

                            @blk.scalar
                            def _(s):
                                for m in range(FGN):
                                    s.wait_ge(s_mmG, m + 1)
                                    if m >= 2:
                                        s.wait_ge(ssg_sl[m % 2], 16 * ((m - 2) // 2 + 1))
                                    s.activation(gsb[m % 2][:, :], pG[m % 2][:, :],
                                                 AF.Silu).then_inc(s_aG, 1)

                            @blk.sync
                            def _(sy):
                                for m in range(FGN):
                                    sy.wait_ge(s_aG, m + 1)
                                    sy.dma_start(out=sg_d[m * 128:(m + 1) * 128, :],
                                                 in_=gsb[m % 2][:, :]).then_inc(ssg_sl[m % 2], 16)
                                for j in range(2):
                                    sy.wait_ge(ssg_sl[j], 16 * ((FGN - 1 - j) // 2 + 1))

                if "gugu" in ph:
                    with ExitStack() as eU:
                        sbU = lambda n, shp, dt=F32: eU.enter_context(nc.sbuf_tensor(n, shp, dt))
                        wu_sb = [sbU(f"wu{i}", [128, KC * 128], BF16) for i in range(2)]
                        sgl = [sbU(f"sgl{i}", [128, T], BF16) for i in range(2)]
                        h_sb = [sbU(f"hsb{i}", [128, T], BF16) for i in range(2)]
                        with nc.psum_tensor("pU0", [128, T], F32) as pU0, \
                             nc.psum_tensor("pU1", [128, T], F32) as pU1, nc.Block() as blk:
                            pU = [pU0, pU1]

                            @blk.gpsimd
                            def _(g):
                                for m in range(FGN):
                                    if m >= 2:
                                        g.wait_ge(s_mmU, m - 1)
                                    g.dma_start(out=wu_sb[m % 2][:, :],
                                                in_=uwt_d[m]).then_inc(swu_sl[m % 2], 16)
                                for j in range(2):
                                    g.wait_ge(swu_sl[j], 16 * ((FGN - 1 - j) // 2 + 1))

                            @blk.sync
                            def _(sy):
                                for m in range(FGN):
                                    if m >= 2:
                                        sy.wait_ge(s_dU, m - 1)
                                    sy.dma_start(out=sgl[m % 2][:, :],
                                                 in_=sg_d[m * 128:(m + 1) * 128, :]
                                                 ).then_inc(sgl_sl[m % 2], 16)
                                    if m >= 1:
                                        sy.wait_ge(s_dU, m)
                                        for qq in range(NQ):
                                            sy.dma_start(
                                                out=h_d[qq][:, m - 1, :],
                                                in_=h_sb[(m - 1) % 2]
                                                [:, qq * TQ:(qq + 1) * TQ]
                                                ).then_inc(shs_sl[(m - 1) % 2], 16)
                                sy.wait_ge(s_dU, FGN)
                                for qq in range(NQ):
                                    sy.dma_start(out=h_d[qq][:, FGN - 1, :],
                                                 in_=h_sb[(FGN - 1) % 2]
                                                 [:, qq * TQ:(qq + 1) * TQ]
                                                 ).then_inc(shs_sl[(FGN - 1) % 2], 16)
                                for j in range(2):
                                    sy.wait_ge(shs_sl[j], 64 * ((FGN - 1 - j) // 2 + 1))
                                for j in range(2):
                                    sy.wait_ge(sgl_sl[j], 16 * ((FGN - 1 - j) // 2 + 1))

                            @blk.tensor
                            def _(te):
                                for m in range(FGN):
                                    te.wait_ge(swu_sl[m % 2], 16 * (m // 2 + 1))
                                    if m >= 2:
                                        te.wait_ge(s_dU, m - 1)
                                    for k in range(KC):
                                        for nb in range(NB):
                                            ins = te.matmul(
                                                pU[m % 2][:, nb * 512:(nb + 1) * 512],
                                                wu_sb[m % 2][:, k * 128:(k + 1) * 128],
                                                xnT(k, nb),
                                                start=(k == 0), stop=(k == KC - 1))
                                    ins.then_inc(s_mmU, 1)

                            @blk.vector
                            def _(v):
                                for m in range(FGN):
                                    v.wait_ge(s_mmU, m + 1)
                                    v.wait_ge(sgl_sl[m % 2], 16 * (m // 2 + 1))
                                    if m >= 2:
                                        v.wait_ge(shs_sl[m % 2], 64 * ((m - 2) // 2 + 1))
                                    v.tensor_mul(h_sb[m % 2][:, :], sgl[m % 2][:, :],
                                                 pU[m % 2][:, :]).then_inc(s_dU, 1)

            if "s" in ph:
                with ExitStack() as eS:
                    sbS = lambda n, shp, dt=F32: eS.enter_context(nc.sbuf_tensor(n, shp, dt))
                    xq_sb = sbS("xq_sb", [128, KC * TQ], BF16)
                    ug_sb = [sbS(f"ug{i}", [128, KC * 256], BF16) for i in range(2)]
                    tmp_sb = sbS("tmp_sb", [128, NV * SGN * TQ], BF16)
                    ws_sb = [sbS(f"wssb{i}", [128, NV * SGN * 128], BF16) for i in range(3)]
                    delta = sbS("delta", [128, TQ]); absd = sbS("absd", [128, TQ])
                    ldl = sbS("ldl", [128, TQ]); gesb = sbS("gesb", [128, TQ])
                    negf = sbS("negf", [128, TQ]); keep = sbS("keep", [128, TQ])
                    au_f = sbS("au_f", [128, TQ]); au_s = sbS("au_s", [128, TQ])
                    au_k = sbS("au_k", [128, TQ]); au_sk = sbS("au_sk", [128, TQ])
                    dt_sc = [sbS(f"dtsc{i}", [128, TQ]) for i in range(2)]
                    fst = [sbS(f"fst{i}", [128, TQ]) for i in range(2)]
                    selmap = {1: au_s, 2: au_f, 3: au_s, 4: au_f, 5: au_sk, 6: au_k}
                    with nc.psum_tensor("pS1u", [128, 2 * TQ], F32) as pS1u, \
                         nc.psum_tensor("pS1g", [128, 2 * TQ], F32) as pS1g, \
                         nc.psum_tensor("pF", [128, 2 * TQ], F32) as pF:
                        s_cc = {"v": 0}
                        with nc.Block() as blk:
                            for q in range(NQ):
                                sq = s_sems
                                qo = {k: q * PERQ[k] for k in PERQ}
                                @blk.sync
                                def _(sy, q=q, sq=sq, qo=qo):
                                    if q >= 1:
                                        sy.wait_ge(sq["m1"], q * PERQ["m1"])
                                    for piece in range(4):
                                        sy.dma_start(
                                            out=xq_sb.rearrange("p (c n) -> p c n", c=KC)
                                            [:, piece * 8:(piece + 1) * 8, :],
                                            in_=xnT_d[q][:, piece * 8:(piece + 1) * 8, :]
                                            ).then_inc(sxq_sl[piece], 16)
                                    if q == NQ - 1:
                                        for j in range(4):
                                            sy.wait_ge(sxq_sl[j], 16 * NQ)

                                @blk.gpsimd
                                def _(g, q=q, sq=sq, qo=qo):
                                    sgs = range(SGN) if q == 0 else range(2, SGN)
                                    for sg in sgs:
                                        w = qo["m1"] + sg - 1
                                        if w > 0:
                                            g.wait_ge(sq["m1"], w)
                                        gsg = q * SGN + sg
                                        g.dma_start(out=ug_sb[sg % 2][:, :],
                                                    in_=ugi_d[sg]).then_inc(sug_sl[gsg % 2], 16)
                                    if q + 1 < NQ:
                                        for sg in range(2):
                                            g.wait_ge(sq["m1"], qo["m1"] + sg + 7)
                                            gsg = (q + 1) * SGN + sg
                                            g.dma_start(out=ug_sb[sg % 2][:, :],
                                                        in_=ugi_d[sg]).then_inc(sug_sl[gsg % 2], 16)
                                    for m in range(3):
                                        w = qo["m3"] + m - 2
                                        if w > 0:
                                            g.wait_ge(sq["m3"], w)
                                        mg = q * MGN + m
                                        g.dma_start(out=ws_sb[m % 3][:, :],
                                                    in_=ws_d[m]).then_inc(sws_sl[mg % 3], 16)
                                    for m in range(MGN):
                                        if m + 3 < MGN:
                                            g.wait_ge(sq["m3"], qo["m3"] + m + 1)
                                            mg = q * MGN + m + 3
                                            g.dma_start(out=ws_sb[(m + 3) % 3][:, :],
                                                        in_=ws_d[m + 3]).then_inc(sws_sl[mg % 3], 16)
                                        g.wait_ge(sq["e3"], qo["e3"] + m + 1)
                                        mg = q * MGN + m
                                        g.dma_start(out=par_d[m * 128:(m + 1) * 128,
                                                              q * TQ:(q + 1) * TQ],
                                                    in_=fst[m % 2][:, :],
                                                    accum_op=ALU.add).then_inc(sac_sl[mg % 2], 16)
                                    if q == NQ - 1:
                                        for j in range(2):
                                            g.wait_ge(sac_sl[j], 16 * (NQ * MGN // 2))
                                            g.wait_ge(sws_sl[j], 16 * ((NQ * MGN - 1 - j) // 3 + 1))
                                            g.wait_ge(sug_sl[j], 16 * (NQ * SGN // 2))
                                        g.wait_ge(sws_sl[2], 16 * ((NQ * MGN - 3) // 3 + 1))

                                @blk.tensor
                                def _(te, q=q, sq=sq, qo=qo):
                                    for sg in range(SGN):
                                        gsg = q * SGN + sg
                                        te.wait_ge(sug_sl[gsg % 2], 16 * (gsg // 2 + 1))
                                        w = qo["e1"] + sg - 1
                                        if w > 0:
                                            te.wait_ge(sq["e1"], w)
                                        par = (sg % 2) * TQ
                                        for k in range(KC):
                                            if sg == 0 and k % 8 == 0:
                                                te.wait_ge(sxq_sl[k // 8], 16 * (q + 1))
                                            ins = te.matmul(
                                                pS1u[:, par:par + TQ],
                                                ug_sb[sg % 2][:, k * 256:k * 256 + 128],
                                                xq_sb[:, k * TQ:(k + 1) * TQ],
                                                start=(k == 0), stop=(k == KC - 1))
                                        for k in range(KC):
                                            ins = te.matmul(
                                                pS1g[:, par:par + TQ],
                                                ug_sb[sg % 2][:, k * 256 + 128:(k + 1) * 256],
                                                xq_sb[:, k * TQ:(k + 1) * TQ],
                                                start=(k == 0), stop=(k == KC - 1))
                                        ins.then_inc(sq["m1"], 1)
                                    te.wait_ge(sq["ddt"], qo["ddt"] + 6 * SGN)
                                    te.wait_ge(s_zm, 1)
                                    for m in range(MGN):
                                        mg = q * MGN + m
                                        te.wait_ge(sws_sl[mg % 3], 16 * (mg // 3 + 1))
                                        w = qo["e3"] + m - 1
                                        if w > 0:
                                            te.wait_ge(sq["e3"], w)
                                        par = (m % 2) * TQ
                                        for c in range(NV * SGN):
                                            te.matmul(pF[:, par:par + TQ],
                                                      ws_sb[m % 3][:, c * 128:(c + 1) * 128],
                                                      tmp_sb[:, c * TQ:(c + 1) * TQ],
                                                      start=(c == 0), stop=False)
                                        te.matmul(pF[:, par:par + TQ],
                                                  lb_sb[0:32, m * 128:(m + 1) * 128],
                                                  zm_sb[0:32, q * TQ:(q + 1) * TQ],
                                                  start=False, stop=True).then_inc(sq["m3"], 1)

                                @blk.scalar
                                def _(s, q=q, sq=sq, qo=qo):
                                    for sg in range(SGN):
                                        s.wait_ge(sq["cld"], qo["cld"] + sg + 1)
                                        s.activation(ldl[:, :], absd[:, :],
                                                     AF.Ln).then_inc(sq["lnd"], 1)
                                        s.wait_ge(sq["lnd"], qo["lnd"] + sg + 1)
                                        for v_ in range(1, 7):
                                            w = max(0, 6 * sg + v_ - 2)
                                            if qo["ddt"] + w > 0:
                                                s.wait_ge(sq["ddt"], qo["ddt"] + w)
                                            s.activation(dt_sc[v_ % 2][:, :], ldl[:, :], AF.Exp,
                                                         bias=nd_sb[:, v_ - 1:v_],
                                                         scale=msc_sb[:, v_ - 1:v_]
                                                         ).then_inc(sq["adt"], 1)

                                @blk.vector
                                def _(v, q=q, sq=sq, qo=qo):
                                    if q >= 1:
                                        v.wait_ge(sq["m3"], q * PERQ["m3"])
                                    def bump(ins):
                                        s_cc["v"] += 1
                                        ins.then_inc(sq["chv"], 1)
                                        v.wait_ge(sq["chv"], qo["chv"] + s_cc["v"])
                                    for sg in range(SGN):
                                        par = (sg % 2) * TQ
                                        v.wait_ge(sq["m1"], qo["m1"] + sg + 1)
                                        bump(v.tensor_scalar(delta[:, :], pS1g[:, par:par + TQ],
                                                             lp_sb[:, sg:sg + 1], None,
                                                             ALU.subtract))
                                        v.tensor_copy(au_f[:, :], pS1u[:, par:par + TQ]
                                                      ).then_inc(sq["e1"], 1)
                                        v.wait_ge(sq["e1"], qo["e1"] + sg + 1)
                                        v.tensor_copy(tmp_sb[:, (0 * SGN + sg) * TQ:
                                                             (0 * SGN + sg + 1) * TQ],
                                                      au_f[:, :])
                                        if sg >= 1:
                                            v.wait_ge(sq["lnd"], qo["lnd"] + sg)
                                        v.scalar_tensor_tensor(absd[:, :], delta[:, :], -1.0,
                                                               delta[:, :], ALU.mult, ALU.max
                                                               ).then_inc(sq["cld"], 1)
                                        v.wait_ge(sq["cld"], qo["cld"] + sg + 1)
                                        bump(v.tensor_single_scalar(gesb[:, :], delta[:, :],
                                                                    0.0, ALU.is_gt))
                                        bump(v.tensor_scalar(negf[:, :], gesb[:, :], 2.0, -1.0,
                                                             ALU.mult, ALU.add))
                                        bump(v.tensor_single_scalar(keep[:, :], absd[:, :],
                                                                    THD, ALU.is_le))
                                        bump(v.tensor_mul(au_s[:, :], au_f[:, :], negf[:, :]))
                                        bump(v.tensor_mul(au_k[:, :], au_f[:, :], keep[:, :]))
                                        bump(v.tensor_mul(au_sk[:, :], au_s[:, :], keep[:, :]))
                                        for v_ in range(1, 7):
                                            v.wait_ge(sq["adt"], qo["adt"] + 6 * sg + v_)
                                            v.tensor_mul(tmp_sb[:, (v_ * SGN + sg) * TQ:
                                                                (v_ * SGN + sg + 1) * TQ],
                                                         dt_sc[v_ % 2][:, :],
                                                         selmap[v_][:, :]).then_inc(sq["ddt"], 1)
                                    for m in range(MGN):
                                        v.wait_ge(sq["m3"], qo["m3"] + m + 1)
                                        mg = q * MGN + m
                                        if mg >= 2:
                                            v.wait_ge(sac_sl[mg % 2], 16 * ((mg - 2) // 2 + 1))
                                        par = (m % 2) * TQ
                                        v.tensor_copy(fst[m % 2][:, :], pF[:, par:par + TQ]
                                                      ).then_inc(sq["e3"], 1)

        if "down" in ph:
            with ExitStack() as eD:
                sbD = lambda n, shp, dt=F32: eD.enter_context(nc.sbuf_tensor(n, shp, dt))
                hh_sb = sbD("hh_sb", [128, FGN * TQ], BF16)
                dw_sb = [sbD(f"dwsb{i}", [128, FGN * 128], BF16) for i in range(4)]
                pp_sb = sbD("pp_sb", [128, TQ])
                ot_sb = sbD("ot_sb", [128, TQ])
                oct_sb = [sbD(f"oct{i}", [128, 4 * 1024]) for i in range(2)]
                with nc.psum_tensor("pD", [128, 2 * TQ], F32) as pD, \
                     nc.psum_tensor("pN", [128, 2 * TQ], F32) as pN:
                    with nc.Block() as blk:
                      for qd in range(4):
                        sd = d_sems
                        do = {k: qd * PERD[k] for k in PERD}
                        if True:
                            @blk.sync
                            def _(sy, qd=qd, sd=sd, do=do):
                                if qd >= 1:
                                    sy.wait_ge(sd["md"], qd * PERD["md"])
                                for jj in range(2):
                                    sy.dma_start(
                                        out=hh_sb.rearrange("p (c n) -> p c n", c=FGN)
                                        [:, jj * 39:(jj + 1) * 39, :],
                                        in_=h_d[qd][:, jj * 39:(jj + 1) * 39, :]
                                        ).then_inc(shl_sl[jj], 16)
                                for m in range(MGN):
                                    if do["ed"] + m > 0:
                                        sy.wait_ge(sd["ed"], do["ed"] + m)
                                    sy.dma_start(out=pp_sb[:, :],
                                                 in_=par_d[m * 128:(m + 1) * 128,
                                                           qd * TQ:(qd + 1) * TQ]
                                                 ).then_inc(sd["pld"], 16)
                                if qd == 3:
                                    sy.wait_ge(sd["pld"], do["pld"] + 16 * MGN)
                                    for j in range(2):
                                        sy.wait_ge(shl_sl[j], 16 * NQ)

                            @blk.gpsimd
                            def _(g, qd=qd, sd=sd, do=do):
                                def dwload(m):
                                    w = do["md"] + m - 3
                                    if w > 0:
                                        g.wait_ge(sd["md"], w)
                                    gm = qd * MGN + m
                                    g.dma_start(out=dw_sb[gm % 4][:, :],
                                                in_=dwT_d[m]).then_inc(sdw_sl[gm % 4], 16)
                                for m in range(4):
                                    dwload(m)
                                if qd == 0:
                                    g.wait_ge(s_pre, 16)
                                for m in range(MGN):
                                    if m + 4 < MGN:
                                        dwload(m + 4)
                                    if m % 8 == 7:
                                        o = m // 8
                                        og = qd * 4 + o
                                        g.wait_ge(sd["en"], do["en"] + m + 1)
                                        g.dma_start(
                                            out=out_d.rearrange("(a j p) h -> a p j h",
                                                                j=4, p=128)
                                            [qd][:, :, o * 1024:(o + 1) * 1024],
                                            in_=oct_sb[o % 2].rearrange("p (j h) -> p j h",
                                                                        j=4),
                                            accum_op=ALU.add).then_inc(sou_sl[og % 2], 16)
                                if qd == 3:
                                    for j in range(2):
                                        g.wait_ge(sou_sl[j], 16 * (NQ * 4 // 2))
                                    for j in range(4):
                                        g.wait_ge(sdw_sl[j], 16 * (NQ * MGN // 4))

                            @blk.tensor
                            def _(te, qd=qd, sd=sd, do=do):
                                te.wait_ge(shl_sl[0], 16 * (qd + 1))
                                for m in range(MGN):
                                    gm = qd * MGN + m
                                    te.wait_ge(sdw_sl[gm % 4], 16 * (gm // 4 + 1))
                                    par = (m % 2) * TQ
                                    if do["ed"] + m - 1 > 0:
                                        te.wait_ge(sd["ed"], do["ed"] + m - 1)
                                    for k in range(FGN):
                                        if m == 0 and k == 39:
                                            te.wait_ge(shl_sl[1], 16 * (qd + 1))
                                        ins = te.matmul(
                                            pD[:, par:par + TQ],
                                            dw_sb[(qd * MGN + m) % 4][:, k * 128:(k + 1) * 128],
                                            hh_sb[:, k * TQ:(k + 1) * TQ],
                                            start=(k == 0), stop=(k == FGN - 1))
                                    ins.then_inc(sd["md"], 1)
                                    if m >= 1:
                                        mm = m - 1
                                        te.wait_ge(sd["ed"], do["ed"] + mm + 1)
                                        if do["en"] + mm - 1 > 0:
                                            te.wait_ge(sd["en"], do["en"] + mm - 1)
                                        parn = (mm % 2) * TQ
                                        for j in range(4):
                                            ins = te.transpose(
                                                pN[:, parn + j * 128:parn + (j + 1) * 128],
                                                ot_sb[:, j * 128:(j + 1) * 128], id_sb[:, :])
                                        ins.then_inc(sd["tr"], 1)
                                mm = MGN - 1
                                te.wait_ge(sd["ed"], do["ed"] + mm + 1)
                                if do["en"] + mm - 1 > 0:
                                    te.wait_ge(sd["en"], do["en"] + mm - 1)
                                parn = (mm % 2) * TQ
                                for j in range(4):
                                    ins = te.transpose(pN[:, parn + j * 128:parn + (j + 1) * 128],
                                                       ot_sb[:, j * 128:(j + 1) * 128],
                                                       id_sb[:, :])
                                ins.then_inc(sd["tr"], 1)

                            @blk.vector
                            def _(v, qd=qd, sd=sd, do=do):
                                for m in range(MGN):
                                    par = (m % 2) * TQ
                                    v.wait_ge(sd["md"], do["md"] + m + 1)
                                    v.wait_ge(sd["pld"], do["pld"] + 16 * (m + 1))
                                    if do["tr"] + m > 0:
                                        v.wait_ge(sd["tr"], do["tr"] + m)
                                    v.tensor_add(ot_sb[:, :], pD[:, par:par + TQ],
                                                 pp_sb[:, :]).then_inc(sd["ed"], 1)
                                    v.wait_ge(sd["tr"], do["tr"] + m + 1)
                                    o = m // 8
                                    og = qd * 4 + o
                                    if m % 8 == 0 and og >= 2:
                                        v.wait_ge(sou_sl[og % 2], 16 * ((og - 2) // 2 + 1))
                                    parn = (m % 2) * TQ
                                    v.tensor_copy(
                                        oct_sb[o % 2].rearrange("p (j h) -> p j h", j=4)
                                        [:, :, (m % 8) * 128:(m % 8 + 1) * 128],
                                        pN[:, parn:parn + TQ].rearrange("p (j c) -> p j c",
                                                                        j=4)
                                        ).then_inc(sd["en"], 1)

        with nc.Block() as blk:
            @blk.gpsimd
            def _(g):
                if "down" in ph:
                    for j in range(2):
                        g.wait_ge(sou_sl[j], 16 * (NQ * 4 // 2))
                else:
                    g.wait_ge(s_pre, 16)
                if debug:
                    dsem = nc.semaphore("s_dbg").__enter__()
                    n_dbg = 0
                    if "p0" in ph:
                        g.dma_start(out=xn_dump[:, :], in_=xn_d[:, :]).then_inc(dsem, 16)
                        n_dbg += 1
                    if "gupre" in ph:
                        for qq in range(NQ):
                            g.dma_start(out=xnT_dump.rearrange("(c p) (q n) -> q p c n",
                                                               p=128, q=NQ)[qq],
                                        in_=xnT_d[qq]).then_inc(dsem, 16)
                        n_dbg += 1
                        g.dma_start(out=zm_dump[:, :], in_=zm_sb[:, :]).then_inc(dsem, 16)
                        n_dbg += 1
                    if "gubase" in ph or "s" in ph:
                        g.dma_start(out=par_dump[:, :], in_=par_d[:, :]).then_inc(dsem, 16)
                        n_dbg += 1
                    if "gugu" in ph:
                        for qq in range(NQ):
                            g.dma_start(out=h_dump.rearrange("(c p) (q n) -> q p c n",
                                                             p=128, q=NQ)[qq],
                                        in_=h_d[qq]).then_inc(dsem, 16)
                        n_dbg += 1
                    g.wait_ge(dsem, 16 * n_dbg)

    return nc


def _prep_inputs(L0_attn_hidden_states, L0_attn_up_w, L0_attn_gate_w,
                 L0_attn_local_point, L0_attn_local_approx_output,
                 L0_attn_fuse_weight, L0_attn_discount_factor,
                 L0_attn_ffn_gate_w, L0_attn_ffn_up_w, L0_attn_ffn_down_w,
                 L0_ffn_W_up, L0_ffn_gate_w, L0_ffn_expert_bias,
                 L0_ffn_lora_A, L0_ffn_lora_B):
    f32 = np.float32
    asc = np.ascontiguousarray

    def tile_panels(SRC, W):
        Ktot, Ntot = SRC.shape
        kc, nm = Ktot // 128, Ntot // W
        return asc(SRC.reshape(kc, 128, nm, W).transpose(2, 1, 0, 3).reshape(nm, 128, kc * W))

    x = asc(np.asarray(L0_attn_hidden_states, f32).reshape(NCORES * T, H))
    wupT = tile_panels(np.asarray(L0_ffn_W_up, f32).T, 128)
    upT = np.asarray(L0_attn_up_w, f32).T
    gateT = np.asarray(L0_attn_gate_w, f32).T
    ugi = tile_panels(np.stack([upT.reshape(H, SGN, 128), gateT.reshape(H, SGN, 128)],
                               axis=2).reshape(H, 2 * S_), 256)
    gwt = tile_panels(np.asarray(L0_attn_ffn_gate_w, f32).T, 128)
    uwt = tile_panels(np.asarray(L0_attn_ffn_up_w, f32).T, 128)
    dwT = tile_panels(np.asarray(L0_attn_ffn_down_w, f32).T, 128)
    laoT = np.asarray(L0_attn_local_approx_output, f32).T
    fuseT = np.asarray(L0_attn_fuse_weight, f32).transpose(2, 1, 0)
    ws = tile_panels(np.concatenate([laoT[None], fuseT], axis=0).reshape(NV * S_, H), 128)
    laT = tile_panels(np.asarray(L0_ffn_lora_A, f32).reshape(32, H).T, 32)[0]
    lbT = asc(np.asarray(L0_ffn_lora_B, f32).transpose(0, 2, 1).reshape(32, H))
    g4T = tile_panels(np.asarray(L0_ffn_gate_w, f32).T, 4)[0]
    lp = asc(np.asarray(L0_attn_local_point, f32).reshape(SGN, 128).T)
    disc = asc(np.asarray(L0_attn_discount_factor, f32).reshape(1, 6))
    eb = asc(np.asarray(L0_ffn_expert_bias, f32).reshape(4, 1))
    idm = np.eye(128, dtype=f32)
    onesr = np.ones((1, 128), dtype=f32)
    shared = dict(wupT=wupT, ugi=ugi, gwt=gwt, uwt=uwt, dwT=dwT, ws=ws, laT=laT, lbT=lbT,
                  g4T=g4T, lp=lp, disc=disc, eb=eb, idm=idm, onesr=onesr)
    in_maps = []
    for c in range(NCORES):
        d = dict(shared)
        d["x"] = asc(x[c * T:(c + 1) * T, :])
        in_maps.append(d)
    return in_maps


def kernel(**inputs):
    in_maps = _prep_inputs(**inputs)
    nc = build_program()
    res = run_bass_kernel_spmd(nc, in_maps, core_ids=list(range(NCORES)))
    outs = [np.asarray(r["out"]) for r in res.results]
    full = np.concatenate(outs, axis=0)
    return full.reshape(4, 4096, 4096).astype(np.float32)
